# version 01 (final)
# baseline (speedup 1.0000x reference)
import contextlib
import numpy as np
import concourse.bass as bass
import concourse.mybir as mybir
from concourse.bass_utils import run_bass_kernel_spmd

F32 = mybir.dt.float32
BF16 = mybir.dt.bfloat16
AF = mybir.ActivationFunctionType
OP = mybir.AluOpType
AX = mybir.AxisListType

MGC = 12582912.0
EPS = 1e-5
NCORES = 8
GROUP = 4
TILE = 128 * 128


def configure(d=2048, ho=8192, t=1024):
    global D, DC, HO, HOC, T, SEQ, UCN, DKV, NH, NQ, DH, INV_D, INV_HO
    global W1SL, W2SL, NL1, NL2, NG2, NJ1, W1_PART, W2_PART, WG_PER
    global NHF, HW, TT_N, SW
    D, HO, T = d, ho, t
    DC, HOC = D // 128, HO // 128
    SEQ = GROUP * T
    UCN = SEQ // 128
    DKV, NH, NQ = D // 4, 8, 4
    DH = DKV // NH
    INV_D, INV_HO = 1.0 / D, 1.0 / HO
    W1SL, W2SL = HO // NCORES, D // NCORES
    NL1, NL2 = W1SL // 128, W2SL // 128
    NG2 = HO // 512
    NJ1 = (W1SL + 511) // 512
    W1_PART = NL1 * DC * TILE
    W2_PART = NL2 * HOC * TILE
    WG_PER = W1_PART + W2_PART
    NHF = max(1, T // 512)
    HW = T // NHF
    TT_N = T // 128
    SW = max(T, W1SL, 4 * W2SL)
    global DKC, VGW, KV_K, KV_V, KV_N, KV_PER, NTG, GW
    DKC = DKV // 128
    VGW = NH * (DH + 1)
    KV_K = DKV * T
    KV_V = 128 * TT_N * VGW
    KV_N = NH * T
    KV_PER = KV_K + KV_V + KV_N
    NTG = (TT_N + 3) // 4
    GW = min(T, 512)


configure()
SKIP = set()
S1CUT = 0
DBG = False
HOOK = None


class Ctr:
    def __init__(self, h):
        self.h = h
        self.v = 0

    def inc(self, instr, k=1):
        instr.then_inc(self.h, k)
        self.v += k


class Slots:

    def __init__(self, sems):
        self.s = sems
        self.n = len(sems)
        self._base = [0] * self.n

    def snap(self):
        self._base = [c.v for c in self.s]

    def inc(self, instr, chunk):
        self.s[chunk % self.n].inc(instr, 16)

    def wait(self, eng, chunk, k=1):
        sl = chunk % self.n
        eng.wait_ge(self.s[sl].h, 16 * k * (chunk // self.n + 1) + self._base[sl])

    def wait_all(self, eng):
        for c in self.s:
            eng.wait_ge(c.h, c.v)

    def wait_snap(self, eng):
        for c, b in zip(self.s, self._base):
            eng.wait_ge(c.h, b)


def build_nc():
    nc = bass.Bass("TRN2", target_bir_lowering=False, debug=False,
                   detect_race_conditions=not DBG and HOOK is None)

    x_fm = nc.dram_tensor("x_fm", [D, T], F32, kind="ExternalInput").ap()
    w1s = nc.dram_tensor("w1s", [D, W1SL], F32, kind="ExternalInput").ap()
    w2s = nc.dram_tensor("w2s", [HO, W2SL], F32, kind="ExternalInput").ap()
    wqT = nc.dram_tensor("wqT", [D, D], F32, kind="ExternalInput").ap()
    wkT = nc.dram_tensor("wkT", [D, DKV], F32, kind="ExternalInput").ap()
    wvT = nc.dram_tensor("wvT", [D, DKV], F32, kind="ExternalInput").ap()
    memw = nc.dram_tensor("memw", [DKV, DKV], F32, kind="ExternalInput").ap()
    mnw = nc.dram_tensor("mnw", [DKV, 1], F32, kind="ExternalInput").ap()
    mww = nc.dram_tensor("mww", [1, 1], F32, kind="ExternalInput").ap()
    ident_f = nc.dram_tensor("ident_f", [128, 128], F32, kind="ExternalInput").ap()
    ident_b = nc.dram_tensor("ident_b", [128, 128], BF16, kind="ExternalInput").ap()
    ones_cb = nc.dram_tensor("ones_cb", [128, 1], BF16, kind="ExternalInput").ap()
    ones_cf = nc.dram_tensor("ones_cf", [128, 1], F32, kind="ExternalInput").ap()
    ones_rf = nc.dram_tensor("ones_rf", [1, 128], F32, kind="ExternalInput").ap()
    y_outd = nc.dram_tensor("y_out", [D, T], F32, kind="ExternalOutput").ap()

    wg_in_h = nc.dram_tensor("wg_in", [1, WG_PER], BF16)
    wg_out_h = nc.dram_tensor("wg_out", [NCORES, WG_PER], BF16)
    sc_in_h = nc.dram_tensor("sc_in", [1, W1SL + W2SL], F32)
    sc_out_h = nc.dram_tensor("sc_out", [NCORES, W1SL + W2SL], F32)
    wg_in, wg_out = wg_in_h.ap(), wg_out_h.ap()
    sc_in, sc_out = sc_in_h.ap(), sc_out_h.ap()
    g_spill = nc.dram_tensor("g_spill", [HOC, 128, T], F32).ap()
    kv_in_h = nc.dram_tensor("kv_in", [1, KV_PER], BF16)
    kv_out_h = nc.dram_tensor("kv_out", [GROUP, KV_PER], BF16)
    kv_in, kv_out = kv_in_h.ap(), kv_out_h.ap()
    hsum_d = nc.dram_tensor("hsum_d", [128, 128 // DH], BF16, kind="ExternalInput").ap()
    dbg_d = nc.dram_tensor("dbg", [1, 8 * T], F32, kind="ExternalOutput").ap() if DBG else None
    dbg2_d = nc.dram_tensor("dbg2", [4, 128, T], F32, kind="ExternalOutput").ap() if DBG else None

    est = contextlib.ExitStack()
    nc._est = est
    sb = lambda name, shape, dt: est.enter_context(nc.sbuf_tensor(name, shape, dt))
    psumt = lambda name, shape: est.enter_context(nc.psum_tensor(name, shape, F32))
    mksem = lambda name: Ctr(est.enter_context(nc.semaphore(name)))

    y = sb("y", [128, DC, T], F32)
    idf = sb("idf", [128, 128], F32)
    idb = sb("idb", [128, 128], BF16)
    onescb = sb("onescb", [128, 1], BF16)
    onescf = sb("onescf", [128, 1], F32)
    onesrf = sb("onesrf", [1, 128], F32)
    hsum = sb("hsum", [128, 128 // DH], BF16)
    rows = sb("rows", [1, 4 * SW], F32)
    row = lambda i: rows[0:1, i * SW:i * SW + T]
    roww = lambda i, w: rows[0:1, i * SW:i * SW + w]
    bc = sb("bc", [128, 3 * SW], F32)
    dbg_sb = sb("dbg_sb", [1, 8 * T], F32) if DBG else None
    bcs = lambda i: bc[:, i * SW:i * SW + T]
    bcw = lambda i, w: bc[:, i * SW:i * SW + w]
    sw1c = sb("sw1c", [128, HOC], F32)
    sw2c = sb("sw2c", [128, DC], F32)
    mcols = sb("mcols", [128, max(TT_N, 4)], F32)
    gbuf2 = sb("gbuf2", [128, 2, SW], F32)
    gx = {"t": None}

    def gslot(i, w):
        t = gbuf2 if i < 2 else gx["t"]
        return t[:, i % 2, 0:w]
    xb = {"t": None}

    def xslot(i, w):
        return xb["t"][:, i, 0:w]
    cast4 = sb("cast4", [128, 4, SW], BF16)

    pmm = psumt("pmm", [128, 4, 512])
    putil = psumt("putil", [128, 2, 512])
    pex = psumt("pex", [128, 1, 512])
    pexb = est.enter_context(nc.psum_tensor("pexb", [128, 1024], BF16))
    putil_f = putil[:, :, :].rearrange("p a b -> p (a b)")

    names = ["s_cst", "s_xin", "s_wsc", "s_wabs", "s_wsum", "s_wrow",
             "s_cc", "s_scol", "s_stats", "s_sq", "s_cast", "s_vrow", "s_trow",
             "s_tree", "s_xq", "s_mm1", "s_evac", "s_gelu",
             "s_rnd", "s_mm2", "s_yadd", "s_yout"]
    S = {n: mksem(n) for n in names}
    (s_cst, s_xin, s_wsc, s_wabs, s_wsum, s_wrow, s_cc, s_scol, s_stats,
     s_sq, s_cast, s_vrow, s_trow, s_tree, s_xq, s_mm1, s_evac, s_gelu,
     s_rnd, s_mm2, s_yadd, s_yout) = (S[n] for n in names)
    sl_wdma = Slots([mksem("sl_wdma0"), mksem("sl_wdma1")])
    sl_wtile = Slots([mksem("sl_wtile0"), mksem("sl_wtile1")])
    sl_w1ld = Slots([mksem("sl_w1ld0"), mksem("sl_w1ld1")])
    sl_w2ld = Slots([mksem("sl_w2ld0"), mksem("sl_w2ld1")])
    sl_gout = Slots([mksem(f"sl_gout{i}") for i in range(4)])
    sl_gin = Slots([mksem(f"sl_gin{i}") for i in range(4)])
    sl_gq = Slots([mksem("sl_gq0"), mksem("sl_gq1")])
    sl_kst = Slots([mksem("sl_kst0"), mksem("sl_kst1")])
    sl_vst = Slots([mksem("sl_vst0"), mksem("sl_vst1")])
    s_gaux = mksem("s_gaux")
    s_gdbg = mksem("s_gdbg")

    with nc.Block() as blk:
        def ph_bcast(rowap, slot, wait_sem, wait_val, width=None):
            if width is None:
                width = T
            b_tr = s_trow.v
            nj = (width + 511) // 512

            @blk.tensor
            def _(eng):
                eng.wait_ge(wait_sem.h, wait_val)
                for j in range(nj):
                    w = min(512, width - j * 512)
                    s_trow.inc(eng.matmul(putil[:, j % 2, 0:w], onesrf[:],
                                          rowap[:, j * 512:j * 512 + w],
                                          start=True, stop=True))

            @blk.vector
            def _(eng):
                eng.wait_ge(s_trow.h, b_tr + nj)
                for j in range(nj):
                    w = min(512, width - j * 512)
                    s_vrow.inc(eng.tensor_copy(bcw(slot, width)[:, j * 512:j * 512 + w],
                                               putil[:, j % 2, 0:w]))

        def ph_maxdance(dst_row, wait_sem, wait_val, maxh=None):
            b_tr, b_vr = s_trow.v, s_vrow.v
            ngrp = (TT_N + 3) // 4
            done = 0

            for grp in range(ngrp):
                tiles = list(range(grp * 4, min(TT_N, grp * 4 + 4)))
                bank = grp % 2
                base_done = done

                @blk.tensor
                def _(eng, tiles=tiles, bank=bank, grp=grp, base_done=base_done):
                    if grp == 0:
                        eng.wait_ge(wait_sem.h, wait_val)
                    if grp >= 2:
                        eng.wait_ge(s_vrow.h, b_vr + base_done - len(tiles))
                    for i, j in enumerate(tiles):
                        s_trow.inc(eng.transpose(
                            putil_f[:, bank * 512 + i * 128:bank * 512 + (i + 1) * 128],
                            maxh[:, j * 128:(j + 1) * 128], idf[:]))

                @blk.vector
                def _(eng, tiles=tiles, bank=bank, base_done=base_done):
                    eng.wait_ge(s_trow.h, b_tr + base_done + len(tiles))
                    for i, j in enumerate(tiles):
                        s_vrow.inc(eng.tensor_reduce(
                            out=mcols[:, j:j + 1],
                            in_=putil_f[:, bank * 512 + i * 128:bank * 512 + (i + 1) * 128],
                            op=OP.max, axis=AX.X))

                done += len(tiles)

            b_tr2 = s_trow.v

            @blk.tensor
            def _(eng):
                eng.wait_ge(s_vrow.h, b_vr + TT_N)
                for j in range(TT_N):
                    s_trow.inc(eng.matmul(
                        putil[0:1, j // 4, (j % 4) * 128:(j % 4 + 1) * 128],
                        mcols[:, j:j + 1], idf[:], start=True, stop=True))

            @blk.vector
            def _(eng):
                eng.wait_ge(s_trow.h, b_tr2 + TT_N)
                if TT_N > 4:
                    psrc = putil[0:1, :, :].rearrange("p a b -> p (a b)")[:, 0:T]
                else:
                    psrc = putil[0:1, 0, 0:T]
                s_vrow.inc(eng.tensor_copy(row(dst_row), psrc))

        @blk.sync
        def _(eng):
            for dst, src in ((idf, ident_f), (idb, ident_b), (onescb, ones_cb),
                             (onescf, ones_cf), (onesrf, ones_rf), (hsum, hsum_d)):
                s_cst.inc(eng.dma_start(out=dst[:], in_=src), 16)
            for dc in range(DC):
                s_xin.inc(eng.dma_start(out=y[:, dc, :],
                                        in_=x_fm[dc * 128:(dc + 1) * 128, :]), 16)

        wp_cm = nc.sbuf_tensor("wp", [128, 2, SW], F32)
        wp = wp_cm.__enter__()
        gxp_cm = nc.sbuf_tensor("gxp", [128, 2, SW], F32)
        gx["t"] = gxp_cm.__enter__()
        xbp_cm = nc.sbuf_tensor("xbp", [128, 2, SW], F32)
        xb["t"] = xbp_cm.__enter__()
        b_wabs, b_wsum = s_wabs.v, s_wsum.v
        sl_wdma.snap()

        @blk.sync
        def _(eng):
            for dc in range(DC):
                if dc >= 2:
                    eng.wait_ge(s_wabs.h, b_wabs + dc - 1)
                sl_wdma.inc(eng.dma_start(out=wp[:, dc % 2, 0:W1SL],
                                          in_=w1s[dc * 128:(dc + 1) * 128, :]), dc)

        @blk.scalar
        def _(eng):
            for dc in range(DC):
                sl_wdma.wait(eng, dc)
                if dc >= 4:
                    eng.wait_ge(s_wsum.h, b_wsum + NJ1 * (dc - 3))
                s_wabs.inc(eng.activation(gslot(dc % 4, W1SL), wp[:, dc % 2, 0:W1SL],
                                          AF.Abs, bias=np.float32(0.0),
                                          scale=np.float32(1.0)))

        @blk.tensor
        def _(eng):
            eng.wait_ge(s_cst.h, 96)
            for dc in range(DC):
                eng.wait_ge(s_wabs.h, b_wabs + dc + 1)
                for j in range(NJ1):
                    w = min(512, W1SL - j * 512)
                    s_wsum.inc(eng.matmul(pmm[0:1, j, 0:w], onescf[:],
                                          gslot(dc % 4, W1SL)[:, j * 512:j * 512 + w],
                                          start=(dc == 0), stop=(dc == DC - 1)))

        b_wrow, b_wsum1 = s_wrow.v, s_wsum.v

        @blk.vector
        def _(eng):
            eng.wait_ge(s_wsum.h, b_wsum1)
            if NJ1 > 1:
                msrc = pmm[0:1, 0:NJ1, :].rearrange("p a b -> p (a b)")
            else:
                msrc = pmm[0:1, 0, 0:W1SL]
            eng.tensor_scalar(out=roww(0, W1SL), in0=msrc,
                              scalar1=INV_D, op0=OP.mult, scalar2=EPS, op1=OP.max)
            eng.drain()
            s_wrow.inc(eng.reciprocal(out=roww(1, W1SL), in_=roww(0, W1SL)))
            eng.drain()

        @blk.sync
        def _(eng):
            eng.wait_ge(s_wrow.h, b_wrow + 1)
            s_wsc.inc(eng.dma_start(out=sc_in[0:1, 0:W1SL], in_=roww(0, W1SL)), 16)

        ph_bcast(roww(1, W1SL), 2, s_wrow, b_wrow + 1, width=W1SL)

        b_wabs2, b_vr_w1 = s_wabs.v, s_vrow.v
        sl_wdma.snap()
        sl_wtile.snap()
        w1v = wg_in[0:1, 0:W1_PART].rearrange("o (l r d c) -> (o l) r d c",
                                              l=NL1, r=128, d=DC)

        @blk.sync
        def _(eng):
            eng.wait_ge(s_vrow.h, b_vr_w1)
            for dc in range(DC + 1):
                if dc < DC:
                    if dc >= 2:
                        eng.wait_ge(s_wabs.h, b_wabs2 + 2 * (dc - 1))
                    sl_wdma.inc(eng.dma_start(out=wp[:, dc % 2, 0:W1SL],
                                              in_=w1s[dc * 128:(dc + 1) * 128, :]), dc)
                if dc >= 1:
                    c = dc - 1
                    eng.wait_ge(s_wabs.h, b_wabs2 + 2 * (c + 1))
                    st = cast4[:, c % 2, 0:W1SL].rearrange("p (l c) -> p l c", c=128)
                    sl_wtile.inc(eng.dma_start(
                        out=w1v[:, :, c, :].rearrange("l r c -> r l c"), in_=st), c)

        @blk.vector
        def _(eng):
            for dc in range(DC):
                sl_wdma.wait(eng, dc)
                if dc >= 2:
                    sl_wtile.wait(eng, dc - 2)
                s_wabs.inc(eng.scalar_tensor_tensor(
                    out=xslot(dc % 2, W1SL), in0=wp[:, dc % 2, 0:W1SL],
                    scalar=1.0, op0=OP.mult, in1=bcw(2, W1SL), op1=OP.mult))
                eng.drain()
                eng.tensor_scalar(out=xslot(dc % 2, W1SL), in0=xslot(dc % 2, W1SL),
                                  scalar1=MGC, op0=OP.add, scalar2=MGC, op1=OP.subtract)
                eng.drain()
                s_wabs.inc(eng.tensor_scalar(out=cast4[:, dc % 2, 0:W1SL],
                                             in0=xslot(dc % 2, W1SL),
                                             scalar1=1.0, op0=OP.min,
                                             scalar2=-1.0, op1=OP.max))
                eng.drain()

        b_wabs3, b_wsum3 = s_wabs.v, s_wsum.v
        W2G = 4 * W2SL
        sl_wdma.snap()

        @blk.sync
        def _(eng):
            eng.wait_ge(s_wabs.h, b_wabs3)
            for g in range(NG2):
                if g >= 2:
                    eng.wait_ge(s_wabs.h, b_wabs3 + g - 1)
                src = w2s[g * 512:(g + 1) * 512, :].rearrange("(c p) w -> p c w", p=128)
                sl_wdma.inc(eng.dma_start(
                    out=wp[:, g % 2, 0:W2G].rearrange("p (c w) -> p c w", w=W2SL),
                    in_=src), g)

        @blk.scalar
        def _(eng):
            for g in range(NG2):
                sl_wdma.wait(eng, g)
                if g >= 4:
                    eng.wait_ge(s_wsum.h, b_wsum3 + 4 * (g - 3))
                s_wabs.inc(eng.activation(gslot(g % 4, W2G), wp[:, g % 2, 0:W2G],
                                          AF.Abs, bias=np.float32(0.0),
                                          scale=np.float32(1.0)))

        @blk.tensor
        def _(eng):
            for g in range(NG2):
                eng.wait_ge(s_wabs.h, b_wabs3 + g + 1)
                for c in range(4):
                    s_wsum.inc(eng.matmul(pmm[0:1, 2, 0:W2SL], onescf[:],
                                          gslot(g % 4, W2G)[:, c * W2SL:(c + 1) * W2SL],
                                          start=(g == 0 and c == 0),
                                          stop=(g == NG2 - 1 and c == 3)))

        b_wrow2, b_wsum4 = s_wrow.v, s_wsum.v

        @blk.vector
        def _(eng):
            eng.wait_ge(s_wsum.h, b_wsum4)
            eng.wait_ge(s_wsc.h, 16)
            eng.tensor_scalar(out=roww(0, W2SL), in0=pmm[0:1, 2, 0:W2SL],
                              scalar1=INV_HO, op0=OP.mult, scalar2=EPS, op1=OP.max)
            eng.drain()
            s_wrow.inc(eng.reciprocal(out=roww(1, W2SL), in_=roww(0, W2SL)))
            eng.drain()

        @blk.sync
        def _(eng):
            eng.wait_ge(s_wrow.h, b_wrow2 + 1)
            s_wsc.inc(eng.dma_start(out=sc_in[0:1, W1SL:W1SL + W2SL],
                                    in_=roww(0, W2SL)), 16)

        ph_bcast(roww(1, W2SL), 2, s_wrow, b_wrow2 + 1, width=W2SL)

        b_wabs4, b_vr_w2 = s_wabs.v, s_vrow.v
        sl_wdma.snap()
        sl_wtile.snap()
        w2v = wg_in[0:1, W1_PART:].rearrange("o (l r h c) -> (o l) r h c",
                                             l=NL2, r=128, h=HOC)

        @blk.sync
        def _(eng):
            eng.wait_ge(s_vrow.h, b_vr_w2)
            for g in range(NG2 + 1):
                if g < NG2:
                    if g >= 2:
                        eng.wait_ge(s_wabs.h, b_wabs4 + 5 * (g - 1))
                    src = w2s[g * 512:(g + 1) * 512, :].rearrange("(c p) w -> p c w", p=128)
                    sl_wdma.inc(eng.dma_start(
                        out=wp[:, g % 2, 0:W2G].rearrange("p (c w) -> p c w", w=W2SL),
                        in_=src), g)
                if g >= 1:
                    c = g - 1
                    eng.wait_ge(s_wabs.h, b_wabs4 + 5 * (c + 1))
                    src4 = cast4[:, c % 2, 0:W2G].rearrange("p (i l c) -> p i l c",
                                                            i=4, l=NL2)
                    for l in range(NL2):
                        sl_wtile.inc(eng.dma_start(
                            out=w2v[l, :, 4 * c:4 * c + 4, :],
                            in_=src4[:, :, l, :]), c)

        @blk.vector
        def _(eng):
            sl_wtile.wait_snap(eng)
            for g in range(NG2):
                sl_wdma.wait(eng, g)
                if g >= 2:
                    sl_wtile.wait(eng, g - 2, k=NL2)
                for c in range(4):
                    s_wabs.inc(eng.scalar_tensor_tensor(
                        out=xslot(g % 2, W2G)[:, c * W2SL:(c + 1) * W2SL],
                        in0=wp[:, g % 2, c * W2SL:(c + 1) * W2SL],
                        scalar=1.0, op0=OP.mult, in1=bcw(2, W2SL), op1=OP.mult))
                eng.drain()
                eng.tensor_scalar(out=xslot(g % 2, W2G), in0=xslot(g % 2, W2G),
                                  scalar1=MGC, op0=OP.add, scalar2=MGC, op1=OP.subtract)
                eng.drain()
                s_wabs.inc(eng.tensor_scalar(out=cast4[:, g % 2, 0:W2G],
                                             in0=xslot(g % 2, W2G),
                                             scalar1=1.0, op0=OP.min,
                                             scalar2=-1.0, op1=OP.max))
                eng.drain()

        b_cc = s_cc.v

        @blk.gpsimd
        def _(eng):
            sl_wtile.wait_all(eng)
            eng.wait_ge(s_wsc.h, s_wsc.v)
            s_cc.inc(eng.collective_compute(
                "AllGather", OP.bypass, replica_groups=[list(range(NCORES))],
                ins=[wg_in_h.ap().opt()], outs=[wg_out_h.ap().opt()]))
            s_cc.inc(eng.collective_compute(
                "AllGather", OP.bypass, replica_groups=[list(range(NCORES))],
                ins=[sc_in_h.ap().opt()], outs=[sc_out_h.ap().opt()]))

        xbp_cm.__exit__(None, None, None)
        gxp_cm.__exit__(None, None, None)
        wp_cm.__exit__(None, None, None)
        gx["t"] = None
        xb["t"] = None

        @blk.sync
        def _(eng):
            eng.wait_ge(s_cc.h, b_cc + 2)
            with nc.allow_non_contiguous_dma(reason="tiny one-time scale-column loads"):
                for r in range(NCORES):
                    s_scol.inc(eng.dma_start(
                        out=sw1c[:, r * NL1:(r + 1) * NL1],
                        in_=sc_out[r:r + 1, 0:W1SL].rearrange("o (h p) -> (o p) h", p=128)),
                        16)
                    s_scol.inc(eng.dma_start(
                        out=sw2c[:, r * NL2:(r + 1) * NL2],
                        in_=sc_out[r:r + 1, W1SL:W1SL + W2SL].rearrange("o (h p) -> (o p) h", p=128)),
                        16)

        def ph_ln_stats(first):
            b_cast, b_sq, b_stats, b_vrow = s_cast.v, s_sq.v, s_stats.v, s_vrow.v

            @blk.vector
            def _(eng):
                if first or "ffn1" in SKIP:
                    eng.wait_ge(s_xin.h, 16 * DC)
                    sl_wtile.wait_all(eng)
                    sl_wdma.wait_all(eng)
                for dc in range(DC):
                    if dc >= 2:
                        eng.wait_ge(s_stats.h, b_stats + 2 * NHF * (dc - 1))
                    s_cast.inc(eng.tensor_copy(cast4[:, dc % 2, 0:T], y[:, dc, :]))

            @blk.scalar
            def _(eng):
                for dc in range(DC):
                    eng.wait_ge(s_cast.h, b_cast + dc + 1)
                    if dc >= 2:
                        eng.wait_ge(s_stats.h, b_stats + 2 * NHF * (dc - 1))
                    s_sq.inc(eng.activation(cast4[:, 2 + dc % 2, 0:T], cast4[:, dc % 2, 0:T],
                                            AF.Square, bias=np.float32(0.0),
                                            scale=np.float32(1.0)))

            @blk.tensor
            def _(eng):
                for dc in range(DC):
                    eng.wait_ge(s_sq.h, b_sq + dc + 1)
                    for j in range(NHF):
                        s_stats.inc(eng.matmul(pmm[0:1, j, 0:HW], onescb[:],
                                               cast4[:, dc % 2, j * HW:(j + 1) * HW],
                                               start=(dc == 0), stop=(dc == DC - 1)))
                    for j in range(NHF):
                        s_stats.inc(eng.matmul(pmm[0:1, 2 + j, 0:HW], onescb[:],
                                               cast4[:, 2 + dc % 2, j * HW:(j + 1) * HW],
                                               start=(dc == 0), stop=(dc == DC - 1)))

            @blk.vector
            def _(eng):
                eng.wait_ge(s_stats.h, b_stats + 2 * NHF * DC)
                eng.wait_ge(s_wsc.h, 32)
                if NHF == 2:
                    sum_src = pmm[0:1, 0:2, :].rearrange("p a b -> p (a b)")
                    sq_src = pmm[0:1, 2:4, :].rearrange("p a b -> p (a b)")
                else:
                    sum_src = pmm[0:1, 0, 0:HW]
                    sq_src = pmm[0:1, 2, 0:HW]
                eng.tensor_scalar(out=row(0), in0=sum_src, scalar1=INV_D, op0=OP.mult,
                                  scalar2=None, op1=OP.bypass)
                eng.drain()
                eng.tensor_tensor(out=row(1), in0=row(0), in1=row(0), op=OP.mult)
                eng.drain()
                eng.tensor_scalar(out=row(2), in0=sq_src, scalar1=INV_D, op0=OP.mult,
                                  scalar2=None, op1=OP.bypass)
                eng.drain()
                s_vrow.inc(eng.scalar_tensor_tensor(out=row(2), in0=row(2), scalar=EPS,
                                                    op0=OP.add, in1=row(1),
                                                    op1=OP.subtract))
                eng.drain()

            @blk.scalar
            def _(eng):
                eng.wait_ge(s_vrow.h, b_vrow + 1)
                eng.activation(row(1), row(2), AF.Ln, bias=np.float32(0.0),
                               scale=np.float32(1.0))
                eng.drain()
                s_sq.inc(eng.activation(row(2), row(1), AF.Exp, bias=np.float32(0.0),
                                        scale=np.float32(-0.5)))
                eng.drain()

        def ph_ffn(first):
            gxf_cm = nc.sbuf_tensor(f"gxf{int(first)}", [128, 2, SW], F32)
            gx["t"] = gxf_cm.__enter__()
            xbf_cm = nc.sbuf_tensor(f"xbf{int(first)}", [128, 2, SW], F32)
            xb["t"] = xbf_cm.__enter__()
            mx_cm = nc.sbuf_tensor(f"maxh{int(first)}", [128, T], F32)
            maxh = mx_cm.__enter__()
            ph_ln_stats(first)
            ph_bcast(row(0), 1, s_vrow, s_vrow.v)

            b_vr1, b_xc1, b_abs1, b_tree1 = s_vrow.v, s_cast.v, s_sq.v, s_tree.v

            @blk.vector
            def _(eng):
                eng.wait_ge(s_vrow.h, b_vr1)
                for dc in range(DC + 1):
                    if dc < DC:
                        if dc >= 2:
                            eng.wait_ge(s_sq.h, b_abs1 + dc - 1)
                        s_cast.inc(eng.tensor_tensor(out=xslot(dc % 2, T),
                                                     in0=y[:, dc, :], in1=bcs(1),
                                                     op=OP.subtract))
                    if dc >= 1:
                        c = dc - 1
                        eng.wait_ge(s_sq.h, b_abs1 + c + 1)
                        eng.drain()
                        if c == 0:
                            u = eng.tensor_copy(maxh[:], gslot(0, T))
                        else:
                            u = eng.tensor_tensor(out=maxh[:], in0=gslot(c % 4, T),
                                                  in1=maxh[:], op=OP.max)
                        s_tree.inc(u)
                        eng.drain()

            @blk.scalar
            def _(eng):
                for dc in range(DC):
                    eng.wait_ge(s_cast.h, b_xc1 + dc + 1)
                    if dc >= 4:
                        eng.wait_ge(s_tree.h, b_tree1 + dc - 3)
                    s_sq.inc(eng.activation(gslot(dc % 4, T), xslot(dc % 2, T),
                                            AF.Abs, bias=np.float32(0.0),
                                            scale=np.float32(1.0)))

            ph_maxdance(3, s_tree, s_tree.v, maxh)

            b_vq = s_vrow.v

            @blk.vector
            def _(eng):
                eng.wait_ge(s_sq.h, s_sq.v)
                eng.wait_ge(s_vrow.h, b_vq)
                eng.tensor_tensor(out=row(1), in0=row(2), in1=row(3), op=OP.mult)
                eng.drain()
                eng.tensor_scalar(out=row(1), in0=row(1), scalar1=EPS, op0=OP.max,
                                  scalar2=None, op1=OP.bypass)
                eng.drain()
                eng.tensor_scalar(out=row(0), in0=row(1), scalar1=1.0 / 127.0,
                                  op0=OP.mult, scalar2=None, op1=OP.bypass)
                eng.drain()
                eng.reciprocal(out=row(3), in_=row(1))
                eng.drain()
                eng.tensor_tensor(out=row(3), in0=row(3), in1=row(2), op=OP.mult)
                eng.drain()
                s_vrow.inc(eng.tensor_scalar(out=row(3), in0=row(3), scalar1=127.0,
                                             op0=OP.mult, scalar2=None, op1=OP.bypass))
                eng.drain()

            ph_bcast(row(0), 0, s_vrow, s_vrow.v)
            ph_bcast(row(3), 2, s_vrow, s_vrow.v)

            b_xq, b_vr2 = s_xq.v, s_vrow.v
            wt_cm = nc.sbuf_tensor(f"wtile{int(first)}", [128, 2, DC * 128], BF16)
            wtile = wt_cm.__enter__()
            hb_cm = nc.sbuf_tensor(f"hbuf{int(first)}", [128, 2, T], F32)
            hbuf = hb_cm.__enter__()
            xint_cm = nc.sbuf_tensor(f"x_int{int(first)}", [128, DC, T], BF16)
            x_int = xint_cm.__enter__()

            @blk.vector
            def _(eng):
                eng.wait_ge(s_vrow.h, b_vr2)
                for dc in range(DC):
                    eng.tensor_tensor(out=xslot(dc % 2, T), in0=y[:, dc, :],
                                      in1=bcs(1), op=OP.subtract)
                    eng.drain()
                    eng.scalar_tensor_tensor(out=xslot(dc % 2, T),
                                             in0=xslot(dc % 2, T), scalar=1.0,
                                             op0=OP.mult, in1=bcs(2), op1=OP.mult)
                    eng.drain()
                    s_xq.inc(eng.tensor_scalar(out=x_int[:, dc, :],
                                               in0=xslot(dc % 2, T), scalar1=MGC,
                                               op0=OP.add, scalar2=MGC, op1=OP.subtract))
                    eng.drain()

            b_mm1, b_evac, b_gelu, b_tree2, b_absA = \
                s_mm1.v, s_evac.v, s_gelu.v, s_tree.v, s_sq.v
            sl_w1ld.snap()
            sl_gout.snap()

            @blk.sync
            def _(eng):
                eng.wait_ge(s_cc.h, 2)
                eng.wait_ge(s_mm1.h, b_mm1)
                sl_w2ld.wait_all(eng)
                for hoc in range(HOC):
                    if hoc >= 2:
                        eng.wait_ge(s_mm1.h, b_mm1 + hoc - 1)
                    rk, l = hoc // NL1, hoc % NL1
                    src = wg_out[rk:rk + 1, l * DC * TILE:(l + 1) * DC * TILE]
                    sl_w1ld.inc(eng.dma_start(
                        out=wtile[:, hoc % 2, :],
                        in_=src.rearrange("o (r f) -> (o r) f", r=128)), hoc)

            @blk.tensor
            def _(eng):
                eng.wait_ge(s_xq.h, b_xq + DC)
                for hoc in range(HOC):
                    sl_w1ld.wait(eng, hoc)
                    if hoc >= 2:
                        eng.wait_ge(s_evac.h, b_evac + hoc - 1)
                    for dcj in range(DC):
                        for half in range(NHF):
                            m = eng.matmul(pmm[:, (hoc % 2) * 2 + half, 0:HW],
                                           wtile[:, hoc % 2, dcj * 128:(dcj + 1) * 128],
                                           x_int[:, dcj, half * HW:(half + 1) * HW],
                                           start=(dcj == 0), stop=(dcj == DC - 1))
                    s_mm1.inc(m)

            @blk.vector
            def _(eng):
                eng.wait_ge(s_vrow.h, s_vrow.v)
                eng.wait_ge(s_scol.h, 16 * 2 * NCORES)
                for hoc in range(HOC + 1):
                    if hoc < HOC:
                        eng.wait_ge(s_mm1.h, b_mm1 + hoc + 1)
                        if hoc >= 2:
                            eng.wait_ge(s_gelu.h, b_gelu + hoc - 1)
                        if NHF == 2:
                            mmsrc = pmm[:, (hoc % 2) * 2:(hoc % 2) * 2 + 2, :] \
                                .rearrange("p a b -> p (a b)")
                        else:
                            mmsrc = pmm[:, (hoc % 2) * 2, 0:HW]
                        s_evac.inc(eng.scalar_tensor_tensor(
                            out=hbuf[:, hoc % 2, :], in0=mmsrc,
                            scalar=sw1c[:, hoc:hoc + 1], op0=OP.mult,
                            in1=bcs(0), op1=OP.mult))
                    if hoc >= 1:
                        c = hoc - 1
                        eng.wait_ge(s_sq.h, b_absA + c + 1)
                        eng.drain()
                        if c == 0:
                            u = eng.tensor_copy(maxh[:], xslot(0, T))
                        else:
                            u = eng.tensor_tensor(out=maxh[:], in0=xslot(c % 2, T),
                                                  in1=maxh[:], op=OP.max)
                        s_tree.inc(u)
                        eng.drain()

            @blk.scalar
            def _(eng):
                for hoc in range(HOC):
                    eng.wait_ge(s_evac.h, b_evac + hoc + 1)
                    if hoc >= 4:
                        sl_gout.wait(eng, hoc - 4)
                    s_gelu.inc(eng.activation(gslot(hoc % 4, T), hbuf[:, hoc % 2, :],
                                              AF.Gelu_apprx_tanh, bias=np.float32(0.0),
                                              scale=np.float32(1.0)))
                    eng.drain()
                    if hoc >= 2:
                        eng.wait_ge(s_tree.h, b_tree2 + hoc - 1)
                    s_sq.inc(eng.activation(xslot(hoc % 2, T), gslot(hoc % 4, T),
                                            AF.Abs, bias=np.float32(0.0),
                                            scale=np.float32(1.0)))

            @blk.gpsimd
            def _(eng):
                for hoc in range(HOC):
                    eng.wait_ge(s_gelu.h, b_gelu + hoc + 1)
                    sl_gout.inc(eng.dma_start(out=g_spill[hoc, :, :],
                                              in_=gslot(hoc % 4, T)), hoc)

            xint_cm.__exit__(None, None, None)
            hb_cm.__exit__(None, None, None)
            wt_cm.__exit__(None, None, None)

            ph_maxdance(3, s_tree, s_tree.v, maxh)
            b_vq2 = s_vrow.v

            @blk.vector
            def _(eng):
                eng.wait_ge(s_vrow.h, b_vq2)
                eng.tensor_scalar(out=row(1), in0=row(3), scalar1=EPS, op0=OP.max,
                                  scalar2=None, op1=OP.bypass)
                eng.drain()
                eng.tensor_scalar(out=row(0), in0=row(1), scalar1=1.0 / 127.0,
                                  op0=OP.mult, scalar2=0.5, op1=OP.mult)
                eng.drain()
                eng.reciprocal(out=row(3), in_=row(1))
                eng.drain()
                s_vrow.inc(eng.tensor_scalar(out=row(3), in0=row(3), scalar1=127.0,
                                             op0=OP.mult, scalar2=None, op1=OP.bypass))
                eng.drain()

            ph_bcast(row(3), 1, s_vrow, s_vrow.v)
            ph_bcast(row(0), 2, s_vrow, s_vrow.v)

            QN = 4
            HOQ = HOC // QN
            w2t_cm = nc.sbuf_tensor(f"w2tile{int(first)}", [128, 2, (HOC // 4) * 128],
                                    BF16)
            w2tile = w2t_cm.__enter__()
            hq_cm = nc.sbuf_tensor(f"hq4_{int(first)}", [128, HOQ, T], BF16)
            hq4 = hq_cm.__enter__()
            b_rnd, b_mm2, b_yadd, b_vpb = \
                s_rnd.v, s_mm2.v, s_yadd.v, s_vrow.v
            sl_gin.snap()
            sl_w2ld.snap()

            @blk.gpsimd
            def _(eng):
                sl_gout.wait_all(eng)
                eng.wait_ge(s_tree.h, b_tree2 + HOC)
                for q in range(QN):
                    for i in range(HOQ):
                        gi = q * HOQ + i
                        if gi >= 4:
                            eng.wait_ge(s_rnd.h, b_rnd + gi - 3)
                        sl_gin.inc(eng.dma_start(out=gslot(gi % 4, T),
                                                 in_=g_spill[gi, :, :]), gi)

            @blk.vector
            def _(eng):
                eng.wait_ge(s_vrow.h, b_vpb)
                for q in range(QN):
                    if q >= 1:
                        eng.wait_ge(s_mm2.h, b_mm2 + q * DC)
                    for i in range(HOQ):
                        gi = q * HOQ + i
                        sl_gin.wait(eng, gi)
                        eng.scalar_tensor_tensor(out=xslot(gi % 2, T),
                                                 in0=gslot(gi % 4, T), scalar=1.0,
                                                 op0=OP.mult, in1=bcs(1), op1=OP.mult)
                        eng.drain()
                        s_rnd.inc(eng.tensor_scalar(out=hq4[:, i, :],
                                                    in0=xslot(gi % 2, T),
                                                    scalar1=MGC, op0=OP.add,
                                                    scalar2=MGC, op1=OP.subtract))
                        eng.drain()
                    for dcj in range(DC):
                        eng.wait_ge(s_mm2.h, b_mm2 + q * DC + dcj + 1)
                        if NHF == 2:
                            mmsrc = pmm[:, (dcj % 2) * 2:(dcj % 2) * 2 + 2, :] \
                                .rearrange("p a b -> p (a b)")
                        else:
                            mmsrc = pmm[:, (dcj % 2) * 2, 0:HW]
                        eng.scalar_tensor_tensor(
                            out=xslot(dcj % 2, T), in0=mmsrc,
                            scalar=sw2c[:, dcj:dcj + 1], op0=OP.mult,
                            in1=bcs(2), op1=OP.mult)
                        eng.drain()
                        s_yadd.inc(eng.tensor_tensor(out=y[:, dcj, :], in0=y[:, dcj, :],
                                                     in1=xslot(dcj % 2, T),
                                                     op=OP.add))
                        eng.drain()

            @blk.sync
            def _(eng):
                sl_w1ld.wait_all(eng)
                eng.wait_ge(s_mm1.h, s_mm1.v)
                eng.wait_ge(s_gelu.h, s_gelu.v)
                for q in range(QN):
                    for dcj in range(DC):
                        gq = q * DC + dcj
                        if gq >= 2:
                            eng.wait_ge(s_mm2.h, b_mm2 + gq - 1)
                        rk, l = dcj // NL2, dcj % NL2
                        w2l = wg_out[rk:rk + 1, W1_PART + l * HOC * TILE:
                                     W1_PART + (l + 1) * HOC * TILE] \
                            .rearrange("o (r h c) -> (o r) h c", r=128, h=HOC)
                        sl_w2ld.inc(eng.dma_start(
                            out=w2tile[:, gq % 2, :].rearrange("p (h c) -> p h c", c=128),
                            in_=w2l[:, q * HOQ:(q + 1) * HOQ, :]), gq)

            @blk.tensor
            def _(eng):
                for q in range(QN):
                    eng.wait_ge(s_rnd.h, b_rnd + (q + 1) * HOQ)
                    for dcj in range(DC):
                        gq = q * DC + dcj
                        sl_w2ld.wait(eng, gq)
                        if gq >= 2:
                            eng.wait_ge(s_yadd.h, b_yadd + gq - 1)
                        for i in range(HOQ):
                            for half in range(NHF):
                                m = eng.matmul(pmm[:, (dcj % 2) * 2 + half, 0:HW],
                                               w2tile[:, gq % 2, i * 128:(i + 1) * 128],
                                               hq4[:, i, half * HW:(half + 1) * HW],
                                               start=(i == 0), stop=(i == HOQ - 1))
                        s_mm2.inc(m)

            hq_cm.__exit__(None, None, None)
            w2t_cm.__exit__(None, None, None)
            mx_cm.__exit__(None, None, None)
            xbf_cm.__exit__(None, None, None)
            gxf_cm.__exit__(None, None, None)
            gx["t"] = None
            xb["t"] = None

        def ph_attn():
            HPC = 128 // DH
            ph_ln_stats(False)
            b_vr_a = s_vrow.v

            @blk.vector
            def _(eng):
                eng.wait_ge(s_sq.h, s_sq.v)
                s_vrow.inc(eng.tensor_tensor(out=row(1), in0=row(0), in1=row(2),
                                             op=OP.mult))
                eng.drain()

            ph_bcast(row(2), 0, s_vrow, s_vrow.v)
            ph_bcast(row(1), 1, s_vrow, s_vrow.v)

            qb_cm = nc.sbuf_tensor("qb", [128, DC, T], BF16)
            qb = qb_cm.__enter__()
            w1m_cm = nc.sbuf_tensor("w1m", [128, 1], F32)
            w1mc = w1m_cm.__enter__()
            xba_cm = nc.sbuf_tensor("xba", [128, 2, SW], F32)
            xb["t"] = xba_cm.__enter__()

            z_cm = nc.sbuf_tensor("z", [128, DC, T], BF16)
            z = z_cm.__enter__()
            wqt_cm = nc.sbuf_tensor("wqt", [128, 2, DC * 128], BF16)
            wqt = wqt_cm.__enter__()
            vfm_cm = nc.sbuf_tensor("vfm", [128, DKC, T], BF16)
            vfm = vfm_cm.__enter__()
            kown_cm = nc.sbuf_tensor("kstage", [128, 2, T], BF16)
            kown = kown_cm.__enter__()
            vown_cm = nc.sbuf_tensor("vstage", [128, 2, VGW], BF16)
            vown = vown_cm.__enter__()
            sl_kst.snap()
            sl_vst.snap()

            b_z, b_vr_z = s_cast.v, s_vrow.v
            b_qmm, b_qev = s_mm1.v, s_evac.v
            b_k2s = s_sq.v
            b_aux = s_cc.v
            sl_gq.snap()

            def s1cut(k):
                if S1CUT != k:
                    return False

                @blk.gpsimd
                def _(eng):
                    for c in (s_mm1, s_evac, s_trow, s_vrow, s_sq, s_cast,
                              s_yadd, s_cc):
                        eng.wait_ge(c.h, c.v)
                    sl_gq.wait_all(eng)
                    sl_kst.wait_all(eng)
                    sl_vst.wait_all(eng)
                    sl_gout.wait_all(eng)
                for cm in (vown_cm, kown_cm, vfm_cm, wqt_cm, z_cm):
                    cm.__exit__(None, None, None)
                xba_cm.__exit__(None, None, None)
                xb["t"] = None
                w1m_cm.__exit__(None, None, None)
                qb_cm.__exit__(None, None, None)
                return True

            @blk.vector
            def _(eng):
                eng.wait_ge(s_vrow.h, b_vr_z)
                for dc in range(DC):
                    eng.tensor_tensor(out=xslot(dc % 2, T), in0=y[:, dc, :],
                                      in1=bcs(0), op=OP.mult)
                    eng.drain()
                    s_cast.inc(eng.scalar_tensor_tensor(
                        out=z[:, dc, :], in0=xslot(dc % 2, T), scalar=1.0,
                        op0=OP.mult, in1=bcs(1), op1=OP.subtract))
                    eng.drain()

            if s1cut(1):
                return

            NOC = DC + 2 * DKC

            @blk.gpsimd
            def _(eng):
                sl_w1ld.wait_all(eng)
                sl_w2ld.wait_all(eng)
                eng.wait_ge(s_mm1.h, b_qmm)
                eng.wait_ge(s_mm2.h, s_mm2.v)
                eng.wait_ge(s_yadd.h, s_yadd.v)
                for oc in range(NOC):
                    if oc >= 2:
                        eng.wait_ge(s_mm1.h, b_qmm + oc - 1)
                    if oc < DC:
                        wsrc, col = wqT, oc * 128
                    elif oc < DC + DKC:
                        wsrc, col = wkT, (oc - DC) * 128
                    else:
                        wsrc, col = wvT, (oc - DC - DKC) * 128
                    src = wsrc[:, col:col + 128].rearrange("(c p) w -> p c w", p=128)
                    sl_gq.inc(eng.dma_start(
                        out=wqt[:, oc % 2, :].rearrange("p (c w) -> p c w", w=128),
                        in_=src), oc)

            @blk.tensor
            def _(eng):
                eng.wait_ge(s_cast.h, b_z + DC)
                for oc in range(NOC):
                    sl_gq.wait(eng, oc)
                    if oc >= 2:
                        eng.wait_ge(s_evac.h, b_qev + oc - 1)
                    for dcj in range(DC):
                        for half in range(NHF):
                            m = eng.matmul(pmm[:, (oc % 2) * 2 + half, 0:HW],
                                           wqt[:, oc % 2, dcj * 128:(dcj + 1) * 128],
                                           z[:, dcj, half * HW:(half + 1) * HW],
                                           start=(dcj == 0), stop=(dcj == DC - 1))
                    s_mm1.inc(m)

            @blk.vector
            def _(eng):
                for oc in range(NOC):
                    eng.wait_ge(s_mm1.h, b_qmm + oc + 1)
                    if NHF == 2:
                        mmsrc = pmm[:, (oc % 2) * 2:(oc % 2) * 2 + 2, :] \
                            .rearrange("p a b -> p (a b)")
                    else:
                        mmsrc = pmm[:, (oc % 2) * 2, 0:HW]
                    if oc < DC:
                        s_evac.inc(eng.tensor_copy(qb[:, oc, :], mmsrc))
                    elif oc < DC + DKC:
                        c = oc - DC
                        if c >= 2 and not S1CUT:
                            sl_kst.wait(eng, c - 2)
                            eng.wait_ge(s_sq.h, b_k2s + c - 1)
                        s_evac.inc(eng.tensor_scalar(out=kown[:, c % 2, :],
                                                     in0=mmsrc,
                                                     scalar1=float(1.0 / np.sqrt(DH)),
                                                     op0=OP.mult, scalar2=None,
                                                     op1=OP.bypass))
                    else:
                        c = oc - DC - DKC
                        s_evac.inc(eng.tensor_copy(vfm[:, c, :], mmsrc))
                    eng.drain()

            if s1cut(2):
                return

            b_vtp, b_vev = s_trow.v, s_yadd.v

            @blk.vector
            def _(eng):
                u = eng.memset(vown[:, :, :].rearrange(
                    "p s (h e) -> p s h e", e=DH + 1)[:, :, :, DH:DH + 1], 1.0)
                s_yadd.inc(u)
                eng.drain()

            @blk.tensor
            def _(eng):
                for tt in range(TT_N):
                    for c in range(DKC):
                        n = tt * DKC + c
                        eng.wait_ge(s_evac.h, b_qev + DC + DKC + c + 1)
                        if n >= 1:
                            eng.wait_ge(s_yadd.h, b_vev + 1 + n)
                        s_trow.inc(eng.transpose(
                            pexb[:, (n % 2) * 128:(n % 2) * 128 + 128],
                            vfm[:, c, tt * 128:(tt + 1) * 128], idb[:]))

            @blk.vector
            def _(eng):
                for tt in range(TT_N):
                    for c in range(DKC):
                        n = tt * DKC + c
                        eng.wait_ge(s_trow.h, b_vtp + n + 1)
                        if c == 0 and tt >= 2 and not S1CUT:
                            sl_vst.wait(eng, tt - 2)
                        dstv = vown[:, tt % 2, :].rearrange(
                            "p (h e) -> p h e", e=DH + 1)[:, c * HPC:(c + 1) * HPC, 0:DH]
                        s_yadd.inc(eng.tensor_copy(
                            dstv, pexb[:, (n % 2) * 128:(n % 2) * 128 + 128]
                            .rearrange("p (h e) -> p h e", e=DH)))
                        eng.drain()

            if s1cut(3):
                return

            b_k2t, b_k2v, b_k2w = s_trow.v, s_vrow.v, s_cc.v

            @blk.scalar
            def _(eng):
                for c in range(DKC):
                    eng.wait_ge(s_evac.h, b_qev + DC + c + 1)
                    if c >= 2:
                        eng.wait_ge(s_trow.h, b_k2t + c - 1)
                    s_sq.inc(eng.activation(cast4[:, c % 2, 0:T], kown[:, c % 2, :],
                                            AF.Square, bias=np.float32(0.0),
                                            scale=np.float32(1.0)))

            @blk.tensor
            def _(eng):
                eng.wait_ge(s_evac.h, b_qev + NOC)
                for c in range(DKC):
                    eng.wait_ge(s_sq.h, b_k2s + c + 1)
                    if c >= 2:
                        eng.wait_ge(s_vrow.h, b_k2v + c - 1)
                    for half in range(NHF):
                        m = eng.matmul(pmm[0:HPC, (c % 2) * 2 + half, 0:HW],
                                       hsum[:], cast4[:, c % 2, half * HW:(half + 1) * HW],
                                       start=True, stop=True)
                    s_trow.inc(m)

            sl_gout.snap()

            @blk.vector
            def _(eng):
                for c in range(DKC):
                    eng.wait_ge(s_trow.h, b_k2t + c + 1)
                    if c >= 2 and not S1CUT:
                        sl_gout.wait(eng, c - 2)
                    if NHF == 2:
                        ksrc = pmm[0:HPC, (c % 2) * 2:(c % 2) * 2 + 2, :] \
                            .rearrange("p a b -> p (a b)")
                    else:
                        ksrc = pmm[0:HPC, (c % 2) * 2, 0:HW]
                    s_vrow.inc(eng.tensor_copy(cast4[0:HPC, 2 + c % 2, 0:T], ksrc))
                    eng.drain()

            @blk.gpsimd
            def _(eng):
                for c in range(DKC):
                    eng.wait_ge(s_vrow.h, b_k2v + c + 1)
                    sl_gout.inc(eng.dma_start(
                        out=kv_in[0:1, KV_K + KV_V + c * HPC * T:
                                  KV_K + KV_V + (c + 1) * HPC * T]
                        .rearrange("o (h t) -> (o h) t", h=HPC),
                        in_=cast4[0:HPC, 2 + c % 2, 0:T]), c)

            if s1cut(4):
                return

            @blk.sync
            def _(eng):
                for c in range(DKC):
                    eng.wait_ge(s_evac.h, b_qev + DC + c + 1)
                    sl_kst.inc(eng.dma_start(
                        out=kv_in[0:1, c * 128 * T:(c + 1) * 128 * T]
                        .rearrange("o (p t) -> (o p) t", p=128),
                        in_=kown[:, c % 2, :]), c)
                for tt in range(TT_N):
                    eng.wait_ge(s_yadd.h, b_vev + 1 + (tt + 1) * DKC)
                    sl_vst.inc(eng.dma_start(
                        out=kv_in[0:1, KV_K + tt * 128 * VGW:KV_K + (tt + 1) * 128 * VGW]
                        .rearrange("o (p f) -> (o p) f", p=128),
                        in_=vown[:, tt % 2, :]), tt)

            if s1cut(5):
                return

            grp = [list(range(GROUP)), list(range(GROUP, NCORES))]
            b_ag2 = s_cc.v

            @blk.gpsimd
            def _(eng):
                eng.wait_ge(s_cc.h, b_ag2)
                sl_gout.wait_all(eng)
                sl_kst.wait_all(eng)
                sl_vst.wait_all(eng)
                s_cc.inc(eng.collective_compute(
                    "AllGather", OP.bypass, replica_groups=grp,
                    ins=[kv_in_h.ap().opt()], outs=[kv_out_h.ap().opt()]))
            b_ag_done = s_cc.v

            for cm in (vown_cm, kown_cm, vfm_cm, wqt_cm, z_cm):
                cm.__exit__(None, None, None)

            if "s1only" in SKIP:
                @blk.gpsimd
                def _(eng):
                    eng.wait_ge(s_cc.h, b_ag_done)
                xba_cm.__exit__(None, None, None)
                xb["t"] = None
                w1m_cm.__exit__(None, None, None)
                qb_cm.__exit__(None, None, None)
                return

            memb_cm = nc.sbuf_tensor("memb", [128, DKC * DKC, 128], BF16)
            memb = memb_cm.__enter__()
            mnb_cm = nc.sbuf_tensor("mnb", [128, DKC], F32)
            mnb = mnb_cm.__enter__()
            mwb_cm = nc.sbuf_tensor("mwb", [1, 4], F32)
            mwb = mwb_cm.__enter__()
            b_aux2 = s_gaux.v
            b_mm_done = (s_mm1.v, s_mm2.v, s_trow.v)

            @blk.gpsimd
            def _(eng):
                eng.wait_ge(s_mm1.h, b_mm_done[0])
                eng.wait_ge(s_mm2.h, b_mm_done[1])
                eng.wait_ge(s_trow.h, b_mm_done[2])
                sl_kst.wait_all(eng)
                sl_vst.wait_all(eng)
                for e in range(DKC):
                    s_gaux.inc(eng.dma_start(
                        out=memb[:, e * DKC:(e + 1) * DKC, :],
                        in_=memw[e * 128:(e + 1) * 128, :]
                        .rearrange("p (c w) -> p c w", w=128)), 16)
                with nc.allow_non_contiguous_dma(reason="tiny mn column load"):
                    s_gaux.inc(eng.dma_start(
                        out=mnb[:], in_=mnw.rearrange("(c p) o -> p (c o)", p=128)), 16)
                s_gaux.inc(eng.dma_start(out=mwb[0:1, 0:1], in_=mww), 16)

            b_sig_s, b_sig_v, b_sig_t = s_sq.v, s_vrow.v, s_trow.v

            @blk.scalar
            def _(eng):
                eng.wait_ge(s_gaux.h, b_aux2 + 16 * (DKC + 2))
                s_sq.inc(eng.activation(mwb[0:1, 1:2], mwb[0:1, 0:1], AF.Exp,
                                        bias=np.float32(0.0), scale=np.float32(-1.0)))

            @blk.vector
            def _(eng):
                eng.wait_ge(s_sq.h, b_sig_s + 1)
                eng.tensor_scalar(out=mwb[0:1, 1:2], in0=mwb[0:1, 1:2], scalar1=1.0,
                                  op0=OP.add, scalar2=None, op1=OP.bypass)
                eng.drain()
                eng.reciprocal(out=mwb[0:1, 2:3], in_=mwb[0:1, 1:2])
                eng.drain()
                s_vrow.inc(eng.tensor_scalar(out=mwb[0:1, 3:4], in0=mwb[0:1, 2:3],
                                             scalar1=-1.0, op0=OP.mult, scalar2=1.0,
                                             op1=OP.add))
                eng.drain()

            @blk.tensor
            def _(eng):
                eng.wait_ge(s_vrow.h, b_sig_v + 1)
                s_trow.inc(eng.matmul(putil[:, 0, 0:1], onesrf[:], mwb[0:1, 3:4],
                                      start=True, stop=True))

            @blk.vector
            def _(eng):
                eng.wait_ge(s_trow.h, b_sig_t + 1)
                s_vrow.inc(eng.tensor_copy(w1mc[:], putil[:, 0, 0:1]))
                eng.drain()

            wr_done = s_vrow.v
            am0 = s_yadd.v
            for qi in range(NQ):
                b_sg_v, b_sg_s, b_sg_t, b_am, b_ya = \
                    s_cast.v, s_sq.v, s_trow.v, s_mm2.v, s_yadd.v

                @blk.vector
                def _(eng, qi=qi):
                    if qi == 0:
                        sl_gout.wait_all(eng)
                    for c in range(DKC + 1):
                        if c < DKC:
                            qc = qi * DKC + c
                            if c >= 2:
                                eng.wait_ge(s_sq.h, b_sg_s + c - 1)
                                eng.wait_ge(s_trow.h, b_sg_t + c - 1)
                            eng.tensor_scalar(out=xslot(c % 2, T),
                                              in0=qb[:, qc, :], scalar1=0.0,
                                              op0=OP.min, scalar2=None,
                                              op1=OP.bypass)
                            s_cast.inc(eng.tensor_scalar(out=gslot(c % 2, T),
                                                         in0=qb[:, qc, :], scalar1=0.0,
                                                         op0=OP.max, scalar2=None,
                                                         op1=OP.bypass))
                            eng.drain()
                        if c >= 1:
                            cp = c - 1
                            eng.wait_ge(s_sq.h, b_sg_s + cp + 1)
                            s_cast.inc(eng.scalar_tensor_tensor(
                                out=xslot(cp % 2, T), in0=xslot(cp % 2, T),
                                scalar=-1.0, op0=OP.add, in1=gslot(cp % 2, T),
                                op1=OP.add))
                            eng.drain()
                            s_cast.inc(eng.tensor_copy(cast4[:, cp, 0:T],
                                                       xslot(cp % 2, T)))
                            eng.drain()

                @blk.scalar
                def _(eng, qi=qi):
                    for c in range(DKC):
                        eng.wait_ge(s_cast.h, b_sg_v + max(1, 3 * c - 1))
                        s_sq.inc(eng.activation(xslot(c % 2, T),
                                                xslot(c % 2, T), AF.Exp,
                                                bias=np.float32(0.0),
                                                scale=np.float32(1.0)))

                @blk.tensor
                def _(eng, qi=qi, wr_done=wr_done):
                    eng.wait_ge(s_gaux.h, b_aux2 + 16 * (DKC + 2))
                    eng.wait_ge(s_vrow.h, wr_done)
                    for c in range(DKC):
                        eng.wait_ge(s_cast.h,
                                    b_sg_v + (3 * c + 3 if c < DKC - 1
                                              else 3 * c + 2))
                        for half in range(NHF):
                            m = eng.matmul(
                                putil[0:1, half, 0:HW],
                                mnb[:, c:c + 1],
                                xb["t"][:, c % 2, half * HW:(half + 1) * HW],
                                start=(c == 0), stop=(c == DKC - 1))
                        s_trow.inc(m)
                    if HOOK is not None:
                        import concourse.bass_interp as _bi
                        _bi.add_callback2(
                            eng, (lambda s, inst, qi=qi: HOOK(s, inst, f"den{qi}")),
                            [cast4[:, :, 0:T], mnb[:, :], putil[0:1, 0, 0:HW],
                             qb[:, :, :]])

                @blk.vector
                def _(eng, qi=qi):
                    eng.wait_ge(s_trow.h, b_sg_t + DKC)
                    if NHF == 2:
                        densrc = putil[0:1, :, :].rearrange("p a b -> p (a b)")
                    else:
                        densrc = putil[0:1, 0, 0:T]
                    if DBG:
                        eng.tensor_copy(dbg_sb[0:1, (4 + qi) * T:(5 + qi) * T], densrc)
                        eng.drain()
                        eng.tensor_copy(dbg_sb[0:1, qi * T:(qi + 1) * T],
                                        cast4[0:1, 1, 0:T])
                        eng.drain()
                    eng.reciprocal(out=row(1), in_=densrc)
                    eng.drain()
                    s_vrow.inc(eng.tensor_scalar(out=row(1), in0=row(1),
                                                 scalar1=mwb[0:1, 2:3], op0=OP.mult,
                                                 scalar2=None, op1=OP.bypass))
                    eng.drain()

                if DBG and qi == 2:
                    @blk.gpsimd
                    def _(eng, qi=qi, b_sg_t=b_sg_t):
                        eng.wait_ge(s_trow.h, b_sg_t + 1)
                        for c in range(DKC):
                            eng.dma_start(out=dbg2_d[c, :, :],
                                          in_=cast4[:, c, 0:T]).then_inc(s_gdbg.h, 16)

                ph_bcast(row(1), 2, s_vrow, s_vrow.v)
                b_wr = s_vrow.v
                wr_done = s_vrow.v



                @blk.vector
                def _(eng, qi=qi):
                    eng.wait_ge(s_vrow.h, b_wr)
                    for c in range(DKC):
                        s_cast.inc(eng.tensor_tensor(out=cast4[:, c, 0:T],
                                                     in0=cast4[:, c, 0:T],
                                                     in1=bcs(2), op=OP.mult))
                        eng.drain()
                    if HOOK is not None:
                        import concourse.bass_interp as _bi
                        _bi.add_callback2(
                            eng, (lambda s, inst, qi=qi: HOOK(s, inst, f"mul{qi}")),
                            [cast4[:, :, 0:T], bcs(2), roww(1, T), mwb[0:1, :]])

                @blk.tensor
                def _(eng, qi=qi):
                    eng.wait_ge(s_cast.h, b_sg_v + 4 * DKC)
                    eng.wait_ge(s_gaux.h, b_aux2 + 16 * (DKC + 2))
                    eng.wait_ge(s_evac.h, b_qev + NOC)
                    for dkc in range(DKC):
                        g_am = qi * DKC + dkc
                        if g_am >= 2:
                            eng.wait_ge(s_yadd.h, am0 + g_am - 1)
                        for e in range(DKC):
                            for half in range(NHF):
                                m = eng.matmul(pmm[:, (dkc % 2) * 2 + half, 0:HW],
                                               memb[:, e * DKC + dkc, :],
                                               cast4[:, e, half * HW:(half + 1) * HW],
                                               start=(e == 0), stop=(e == DKC - 1))
                        s_mm2.inc(m)

                @blk.vector
                def _(eng, qi=qi):
                    for dkc in range(DKC):
                        eng.wait_ge(s_mm2.h, b_am + dkc + 1)
                        if NHF == 2:
                            amsrc = pmm[:, (dkc % 2) * 2:(dkc % 2) * 2 + 2, :] \
                                .rearrange("p a b -> p (a b)")
                        else:
                            amsrc = pmm[:, (dkc % 2) * 2, 0:HW]
                        s_yadd.inc(eng.tensor_tensor(out=y[:, qi * DKC + dkc, :],
                                                     in0=y[:, qi * DKC + dkc, :],
                                                     in1=amsrc, op=OP.add))
                        eng.drain()

            b_q2s, b_q2t, b_q2v, b_q2mm = s_sq.v, s_trow.v, s_vrow.v, s_mm2.v
            b_q2ya = s_yadd.v

            @blk.scalar
            def _(eng):
                eng.wait_ge(s_mm2.h, b_q2mm)
                for c in range(DC):
                    if c >= 2:
                        eng.wait_ge(s_trow.h, b_q2t + 2 * (c - 2) + 1)
                    s_sq.inc(eng.activation(cast4[:, c % 2, 0:T], qb[:, c, :],
                                            AF.Square, bias=np.float32(0.0),
                                            scale=np.float32(1.0)))

            @blk.tensor
            def _(eng):
                eng.wait_ge(s_yadd.h, b_q2ya)
                for c in range(DC + 1):
                    if c < DC:
                        eng.wait_ge(s_sq.h, b_q2s + c + 1)
                        if c >= 2:
                            eng.wait_ge(s_vrow.h, b_q2v + c - 1)
                        for half in range(NHF):
                            m = eng.matmul(pmm[0:HPC, (c % 2) * 2 + half, 0:HW],
                                           hsum[:],
                                           cast4[:, c % 2, half * HW:(half + 1) * HW],
                                           start=True, stop=True)
                        s_trow.inc(m)
                    if c >= 1:
                        cp = c - 1
                        eng.wait_ge(s_vrow.h, b_q2v + cp + 1)
                        s_trow.inc(eng.matmul(
                            putil[0:1, 0, cp * HPC:(cp + 1) * HPC],
                            mcols[0:HPC, cp % 4:cp % 4 + 1],
                            idf[0:HPC, 0:HPC], start=True, stop=True))

            @blk.vector
            def _(eng):
                for c in range(DC):
                    eng.wait_ge(s_trow.h, b_q2t + (1 if c == 0 else 2 * c))
                    if NHF == 2:
                        qsrc = pmm[0:HPC, (c % 2) * 2:(c % 2) * 2 + 2, :] \
                            .rearrange("p a b -> p (a b)")
                    else:
                        qsrc = pmm[0:HPC, (c % 2) * 2, 0:HW]
                    s_vrow.inc(eng.tensor_reduce(out=mcols[0:HPC, c % 4:c % 4 + 1],
                                                 in_=qsrc, op=OP.max, axis=AX.X))
                    eng.drain()

            b_q2vf = s_vrow.v

            @blk.vector
            def _(eng):
                eng.wait_ge(s_trow.h, b_q2t + 2 * DC)
                s_vrow.inc(eng.tensor_copy(roww(1, NQ * NH),
                                           putil[0:1, 0, 0:NQ * NH]))
                eng.drain()

            for cm in (mwb_cm, mnb_cm, memb_cm):
                cm.__exit__(None, None, None)
            xba_cm.__exit__(None, None, None)
            xb["t"] = None

            if "qkpv" in SKIP:
                @blk.gpsimd
                def _(eng):
                    eng.wait_ge(s_cc.h, b_ag_done)
                w1m_cm.__exit__(None, None, None)
                qb_cm.__exit__(None, None, None)
                return
            kf_cm = nc.sbuf_tensor("kfull", [128, DKC, SEQ], BF16)
            kfull = kf_cm.__enter__()
            vf_cm = nc.sbuf_tensor("vfull", [128, UCN, VGW], BF16)
            vfull = vf_cm.__enter__()
            cast4f = cast4[:, :, :].rearrange("p a b -> p (a b)")
            k2g = cast4f[0:NH, 0:GROUP * T]
            ebuf = cast4f[:, 0:2 * GW].rearrange("p (s g) -> p s g", g=GW)
            bneg_cm = nc.sbuf_tensor("bneg", [128, NQ * NH], F32)
            bnegc = bneg_cm.__enter__()
            ad_cm = nc.sbuf_tensor("adots", [128, 4 * DH + 4], F32)
            adots = ad_cm.__enter__()
            b_kvl = s_scol.v
            b_sc2 = (s_mm2.v, s_trow.v)

            @blk.sync
            def _(eng):
                eng.wait_ge(s_cc.h, s_cc.v)
                eng.wait_ge(s_mm2.h, b_sc2[0])
                eng.wait_ge(s_trow.h, b_sc2[1])
                for r in range(GROUP):
                    for c in range(DKC):
                        s_scol.inc(eng.dma_start(
                            out=kfull[:, c, r * T:(r + 1) * T],
                            in_=kv_out[r:r + 1, c * 128 * T:(c + 1) * 128 * T]
                            .rearrange("o (p t) -> (o p) t", p=128)), 16)
                    s_scol.inc(eng.dma_start(
                        out=vfull[:, r * TT_N:(r + 1) * TT_N, :],
                        in_=kv_out[r:r + 1, KV_K:KV_K + KV_V]
                        .rearrange("o (t p f) -> (o p) t f",
                                   t=TT_N, p=128)), 16)
                    s_scol.inc(eng.dma_start(
                        out=k2g[:, r * T:(r + 1) * T],
                        in_=kv_out[r:r + 1, KV_K + KV_V:KV_PER]
                        .rearrange("o (h t) -> (o h) t", h=NH)), 16)

            b_bv, b_bt, b_bs = s_vrow.v, s_trow.v, s_sq.v

            @blk.vector
            def _(eng):
                eng.wait_ge(s_scol.h, b_kvl + 16 * GROUP * (DKC + 2))
                s_vrow.inc(eng.tensor_reduce(
                    out=mcols[0:NH, 0:1],
                    in_=k2g[:, :].rearrange("h (r t) -> h r t", r=GROUP),
                    op=OP.max, axis=AX.XY))
                eng.drain()

            @blk.tensor
            def _(eng):
                eng.wait_ge(s_vrow.h, b_bv + 1)
                s_trow.inc(eng.matmul(putil[0:1, 1, 0:NH], mcols[0:NH, 0:1],
                                      idf[0:NH, 0:NH], start=True, stop=True))

            @blk.vector
            def _(eng):
                eng.wait_ge(s_trow.h, b_bt + 1)
                eng.tensor_copy(roww(2, NH), putil[0:1, 1, 0:NH])
                eng.drain()
                for qi in range(NQ):
                    eng.tensor_tensor(
                        out=roww(3, NQ * NH)[:, qi * NH:(qi + 1) * NH],
                        in0=roww(1, NQ * NH)[:, qi * NH:(qi + 1) * NH],
                        in1=roww(2, NH), op=OP.mult)
                    eng.drain()
                s_vrow.inc(eng.tensor_scalar(out=roww(3, NQ * NH),
                                             in0=roww(3, NQ * NH), scalar1=1e-37,
                                             op0=OP.add, scalar2=None, op1=OP.bypass))
                eng.drain()

            @blk.scalar
            def _(eng):
                eng.wait_ge(s_vrow.h, b_bv + 2)
                eng.activation(roww(2, NQ * NH), roww(3, NQ * NH), AF.Ln,
                               bias=np.float32(0.0), scale=np.float32(1.0))
                eng.drain()
                s_sq.inc(eng.activation(roww(3, NQ * NH), roww(2, NQ * NH), AF.Exp,
                                        bias=np.float32(0.0), scale=np.float32(0.5)))

            b_bt2 = s_trow.v

            @blk.vector
            def _(eng):
                eng.wait_ge(s_sq.h, b_bs + 1)
                s_vrow.inc(eng.tensor_scalar(out=roww(3, NQ * NH),
                                             in0=roww(3, NQ * NH), scalar1=-1.0,
                                             op0=OP.mult, scalar2=None, op1=OP.bypass))
                eng.drain()

            @blk.tensor
            def _(eng):
                eng.wait_ge(s_vrow.h, b_bv + 3)
                s_trow.inc(eng.matmul(putil[:, 0, 0:NQ * NH], onesrf[:],
                                      roww(3, NQ * NH), start=True, stop=True))

            b_bv2 = s_vrow.v

            @blk.vector
            def _(eng):
                eng.wait_ge(s_trow.h, b_bt2 + 1)
                s_vrow.inc(eng.tensor_copy(bnegc[:], putil[:, 0, 0:NQ * NH]))
                eng.drain()

            NIT = NQ * NH * NTG
            b_qk, b_ex, b_pv, b_ep, b_tp, b_ya2 = \
                s_mm1.v, s_gelu.v, s_mm2.v, s_vrow.v, s_trow.v, s_yadd.v

            def it_dims(n):
                j, tg = n // NTG, n % NTG
                qi, h = j // NH, j % NH
                gr = qi * DKV + h * DH
                ntt = min(4, TT_N - tg * 4)
                return qi, h, tg, gr // 128, gr % 128, j, ntt

            def emit_pv(eng, g):
                np_, ucp = g // UCN, g % UCN
                qi, h, tg, qc, off, j, ntt = it_dims(np_)
                eng.wait_ge(s_gelu.h, b_ex + g + 1)
                if ucp == 0:
                    eng.wait_ge(s_vrow.h, b_ep + np_)
                for i in range(ntt):
                    m = eng.matmul(pmm[:, i, 0:DH + 1],
                                   ebuf[:, g % 2, i * 128:(i + 1) * 128],
                                   vfull[:, ucp, h * (DH + 1):(h + 1) * (DH + 1)],
                                   start=(ucp == 0), stop=(ucp == UCN - 1))
                s_mm2.inc(m)

            def emit_transp(eng, m_):
                qi, h, tg, qc, off, j, ntt = it_dims(m_)
                eng.wait_ge(s_vrow.h, b_ep + m_ + 1)
                eng.wait_ge(s_yadd.h, b_ya2 + m_)
                for i in range(ntt):
                    t = eng.transpose(pex[0:DH, 0, i * 128:(i + 1) * 128],
                                      adots[:, i * DH:(i + 1) * DH], idf[:])
                s_trow.inc(t)

            @blk.tensor
            def _(eng):
                eng.wait_ge(s_scol.h, b_kvl + 16 * GROUP * (DKC + 2))
                eng.wait_ge(s_vrow.h, b_bv2 + 1)
                if HOOK is not None:
                    import concourse.bass_interp as _bi
                    _bi.add_callback2(
                        eng, (lambda s, inst: HOOK(s, inst, "kvfull")),
                        [vfull[:, :, :], kfull[:, :, :]])
                for n in range(NIT):
                    qi, h, tg, qc, off, j, ntt = it_dims(n)
                    gw = ntt * 128
                    if n >= 1:
                        emit_pv(eng, n * UCN - 1)
                    if n >= 2:
                        emit_transp(eng, n - 2)
                    for uc in range(UCN):
                        g = n * UCN + uc
                        if g >= 2:
                            eng.wait_ge(s_gelu.h, b_ex + g - 1)
                        tp = {"tile_position": (off, 0)} if off == 96 else {}
                        m = eng.matmul(putil[:, g % 2, 0:gw],
                                       kfull[off:off + DH, (h * DH) // 128,
                                             uc * 128:(uc + 1) * 128],
                                       qb[off:off + DH, qc, tg * 512:tg * 512 + gw],
                                       start=True, stop=True, **tp)
                        s_mm1.inc(m)
                        if HOOK is not None and g in (0, 1, UCN, UCN + 1):
                            import concourse.bass_interp as _bi
                            _bi.add_callback2(
                                eng, (lambda s, inst, g=g: HOOK(s, inst, f"qk{g}")),
                                [putil[:, g % 2, 0:gw]])
                        if g >= 1 and uc >= 1:
                            emit_pv(eng, g - 1)
                emit_pv(eng, NIT * UCN - 1)
                if NIT >= 2:
                    emit_transp(eng, NIT - 2)
                emit_transp(eng, NIT - 1)

            @blk.scalar
            def _(eng):
                for g in range(NIT * UCN):
                    n, uc = g // UCN, g % UCN
                    qi, h, tg, qc, off, j, ntt = it_dims(n)
                    gw = ntt * 128
                    eng.wait_ge(s_mm1.h, b_qk + g + 1)
                    if g >= 2:
                        eng.wait_ge(s_mm2.h, b_pv + g - 1)
                    s_gelu.inc(eng.activation(ebuf[:, g % 2, 0:gw],
                                              putil[:, g % 2, 0:gw], AF.Exp,
                                              bias=bnegc[:, j:j + 1],
                                              scale=np.float32(1.0)))
                    if HOOK is not None and g in (0, 1, UCN, UCN + 1):
                        import concourse.bass_interp as _bi
                        _bi.add_callback2(
                            eng, (lambda s, inst, g=g: HOOK(s, inst, f"exp{g}")),
                            [ebuf[:, g % 2, 0:gw], bnegc[:, :]])

            @blk.vector
            def _(eng):
                for m_ in range(NIT):
                    qi, h, tg, qc, off, j, ntt = it_dims(m_)
                    gw = ntt * 128
                    eng.wait_ge(s_mm2.h, b_pv + (m_ + 1) * UCN)
                    if HOOK is not None and m_ == 0:
                        import concourse.bass_interp as _bi
                        _bi.add_callback2(
                            eng, (lambda s, inst: HOOK(s, inst, "pv0")),
                            [pmm[:, 0:ntt, 0:DH + 1]])
                    eng.reciprocal(out=adots[:, 4 * DH:4 * DH + ntt],
                                   in_=pmm[:, 0:ntt, DH:DH + 1]
                                   .rearrange("p a b -> p (a b)"))
                    eng.drain()
                    for i in range(ntt):
                        u = eng.tensor_scalar(out=adots[:, i * DH:(i + 1) * DH],
                                              in0=pmm[:, i, 0:DH],
                                              scalar1=adots[:, 4 * DH + i:4 * DH + i + 1],
                                              op0=OP.mult,
                                              scalar2=w1mc[:], op1=OP.mult)
                    s_vrow.inc(u)
                    eng.drain()
                    eng.wait_ge(s_trow.h, b_tp + m_ + 1)
                    s_yadd.inc(eng.tensor_tensor(
                        out=y[off:off + DH, qc, tg * 512:tg * 512 + gw],
                        in0=y[off:off + DH, qc, tg * 512:tg * 512 + gw],
                        in1=pex[0:DH, 0, 0:gw], op=OP.add))
                    eng.drain()

            for cm in (ad_cm, bneg_cm, vf_cm, kf_cm):
                cm.__exit__(None, None, None)
            w1m_cm.__exit__(None, None, None)
            qb_cm.__exit__(None, None, None)

        if "ffn1" not in SKIP:
            ph_ffn(first=True)
        if "attn" not in SKIP:
            ph_attn()
        if "ffn2" not in SKIP:
            ph_ffn(first=False)

        if DBG:
            @blk.sync
            def _(eng):
                eng.wait_ge(s_yadd.h, s_yadd.v)
                s_yout.inc(eng.dma_start(out=dbg_d, in_=dbg_sb[:]), 16)
        b_yo = s_yout.v

        @blk.sync
        def _(eng):
            eng.wait_ge(s_yadd.h, s_yadd.v)
            for dc in range(DC):
                s_yout.inc(eng.dma_start(out=y_outd[dc * 128:(dc + 1) * 128, :],
                                         in_=y[:, dc, :]), 16)
            eng.wait_ge(s_yout.h, b_yo + 16 * DC)

    nc.finalize()
    return nc



def shard_inputs(x, w1, w2, wq, wk, wv, mem, mn, mw):
    import ml_dtypes
    w1T = np.ascontiguousarray(w1.T)
    w2T = np.ascontiguousarray(w2.T)
    ident = np.eye(128, dtype=np.float32)
    in_maps = []
    for c in range(NCORES):
        b, t0 = c // GROUP, (c % GROUP) * T
        in_maps.append({
            "x_fm": np.ascontiguousarray(x[b, t0:t0 + T, :].T),
            "w1s": np.ascontiguousarray(w1T[:, c * W1SL:(c + 1) * W1SL]),
            "w2s": np.ascontiguousarray(w2T[:, c * W2SL:(c + 1) * W2SL]),
            "wqT": np.ascontiguousarray(wq.T),
            "wkT": np.ascontiguousarray(wk.T),
            "wvT": np.ascontiguousarray(wv.T),
            "memw": mem, "mnw": mn.reshape(DKV, 1), "mww": mw.reshape(1, 1),
            "ident_f": ident, "ident_b": ident.astype(ml_dtypes.bfloat16),
            "hsum_d": (np.arange(128)[:, None] // DH ==
                     np.arange(128 // DH)[None, :]).astype(ml_dtypes.bfloat16),
            "ones_cb": np.ones((128, 1), dtype=ml_dtypes.bfloat16),
            "ones_cf": np.ones((128, 1), dtype=np.float32),
            "ones_rf": np.ones((1, 128), dtype=np.float32),
        })
    return in_maps


def gather_out(results):
    out = np.empty((2, GROUP * T, D), dtype=np.float32)
    for c in range(NCORES):
        b, t0 = c // GROUP, (c % GROUP) * T
        out[b, t0:t0 + T, :] = np.asarray(results[c]["y_out"]).T
    return out


def kernel(**inputs):
    configure()
    in_maps = shard_inputs(
        np.asarray(inputs["L0_attn_x"], dtype=np.float32),
        np.asarray(inputs["L0_ffn_w1"], dtype=np.float32),
        np.asarray(inputs["L0_ffn_w2"], dtype=np.float32),
        np.asarray(inputs["L0_attn_Wq"], dtype=np.float32),
        np.asarray(inputs["L0_attn_Wk"], dtype=np.float32),
        np.asarray(inputs["L0_attn_Wv"], dtype=np.float32),
        np.asarray(inputs["L0_attn_memory"], dtype=np.float32),
        np.asarray(inputs["L0_attn_memory_norm"], dtype=np.float32),
        np.asarray(inputs["L0_attn_memory_weight"], dtype=np.float32),
    )
    nc = build_nc()
    res = run_bass_kernel_spmd(nc, in_maps, core_ids=list(range(NCORES)))
    return gather_out(res.results)
